# version 10
# baseline (speedup 1.0000x reference)
import numpy as np
import ml_dtypes
import concourse.bass as bass
import concourse.mybir as mybir
from concourse.bass_utils import run_bass_kernel_spmd

F32 = mybir.dt.float32
BF16 = mybir.dt.bfloat16
I32 = mybir.dt.int32
AF = mybir.ActivationFunctionType
ALU = mybir.AluOpType
AX = mybir.AxisListType

D = 4096
DFF = 11008
NFC = DFF // 128
NKC = D // 128
SC = 1024
NTC = SC // 128
NSL = 4
ENGINES = ("tensor", "vector", "scalar", "gpsimd", "sync")


class Sem:
    __slots__ = ("name", "h", "n")

    def __init__(self, name):
        self.name = name
        self.h = None
        self.n = 0


class B:
    def __init__(self, nc):
        self.nc = nc
        self.prog = {e: [] for e in ENGINES}
        self.sems = {}
        self.waited = {e: {} for e in ENGINES}

    def sem(self, name):
        s = Sem(name)
        self.sems[name] = s
        return s

    def wait(self, eng, sem, val=None):
        if val is None:
            val = sem.n
        if val <= 0:
            return
        if self.waited[eng].get(sem.name, 0) >= val:
            return
        self.waited[eng][sem.name] = val
        self.prog[eng].append(lambda e, s=sem, v=val: e.wait_ge(s.h, v))

    def run(self, eng, fn, inc=None, k=1):
        if inc is None:
            self.prog[eng].append(fn)
            return None
        inc.n += k

        def f(e, fn=fn, s=inc, k=k):
            fn(e).then_inc(s.h, k)
        self.prog[eng].append(f)
        return inc.n


def build_nc(phases=5, taps=False):
    nc = bass.Bass("TRN2", target_bir_lowering=False)

    xb = nc.dram_tensor("xb", [NSL * SC, D], F32, kind="ExternalInput").ap()
    wqkv_r = nc.dram_tensor("wqkv_r", [24, 128, NKC, 128], F32, kind="ExternalInput").ap()
    bqkv_r = nc.dram_tensor("bqkv_r", [128, 24], F32, kind="ExternalInput").ap()
    wo_r = nc.dram_tensor("wo_r", [8, 128, D], F32, kind="ExternalInput").ap()
    bo_r = nc.dram_tensor("bo_r", [1, D], F32, kind="ExternalInput").ap()
    gate_r = nc.dram_tensor("gate_r", [NFC, 128, NKC, 128], I32, kind="ExternalInput").ap()
    up_r = nc.dram_tensor("up_r", [NFC, 128, NKC, 128], I32, kind="ExternalInput").ap()
    gs_r = nc.dram_tensor("gs_r", [128, NFC], F32, kind="ExternalInput").ap()
    us_r = nc.dram_tensor("us_r", [128, NFC], F32, kind="ExternalInput").ap()
    dwn_r = nc.dram_tensor("dwn_r", [NFC, 128, D], I32, kind="ExternalInput").ap()
    ds_r = nc.dram_tensor("ds_r", [1, D], F32, kind="ExternalInput").ap()
    ident_in = nc.dram_tensor("ident_in", [128, 128], BF16, kind="ExternalInput").ap()
    eps_in = nc.dram_tensor("eps_in", [128, 1], F32, kind="ExternalInput").ap()
    ones16_in = nc.dram_tensor("ones16_in", [1, 128], BF16, kind="ExternalInput").ap()
    out_dr = nc.dram_tensor("out", [SC, D], F32, kind="ExternalOutput").ap()

    kvK = nc.dram_tensor("kvK", [8, 128, NSL * SC], BF16).ap()
    kvV = nc.dram_tensor("kvV", [NSL * NTC, 4, 128, 256], BF16).ap()
    q_dr = nc.dram_tensor("q_dr", [8, 128, SC], BF16).ap()
    op_dr = nc.dram_tensor("op_dr", [NTC, 128, D], F32).ap()
    h_dr = nc.dram_tensor("h_dr", [NFC, 128, SC], BF16).ap()

    tap_list = []
    if taps:
        def tap(name, src_ap, shape, dt):
            t = nc.dram_tensor(name, shape, dt, kind="ExternalOutput").ap()
            tap_list.append((t, src_ap))
        if phases >= 1:
            tap("tap_q", q_dr, [8, 128, SC], BF16)
            tap("tap_k", kvK, [8, 128, NSL * SC], BF16)
            tap("tap_v", kvV, [NSL * NTC, 4, 128, 256], BF16)
        if phases >= 2:
            tap("tap_h", h_dr, [NFC, 128, SC], BF16)
        if phases >= 4:
            tap("tap_op", op_dr, [NTC, 128, D], F32)
    acc_tap = nc.dram_tensor("tap_acc", [128, NTC * 2048], F32, kind="ExternalOutput").ap() if taps and phases >= 5 else None
    ds_tap = nc.dram_tensor("tap_ds", [128, 2048], F32, kind="ExternalOutput").ap() if taps and phases >= 5 else None

    BASE = 16512
    TOP = 229344

    def at(name, shape, dt, off):
        nbytes = int(np.prod(shape[1:])) * mybir.dt.size(dt)
        assert BASE <= off and off + nbytes <= TOP, (name, off, off + nbytes)
        return nc.alloc_sbuf_tensor_at(name, shape, dt, offset=off)[:]

    ident = at("ident", [128, 128], BF16, BASE)
    eps_sb = at("eps_sb", [128, 1], F32, BASE + 256)
    bqkv_sb = at("bqkv_sb", [128, 24], F32, BASE + 288)
    gs_sb = at("gs_sb", [128, NFC], F32, BASE + 384)
    us_sb = at("us_sb", [128, NFC], F32, BASE + 736)
    smalls = at("smalls", [128, 16], F32, BASE + 1088)
    A0 = BASE + 1184

    def sl(i):
        return smalls[:, i:i + 1]

    S_ROW, SSQ, NEGM, E2EPS, MSQ, VAREPS, STD, RSTD, MMR, NEGMAX, SUMEXP, RINV = range(12)

    xbuf = [at(f"xbuf{i}", [128, D], F32, A0 + i * 16384) for i in range(3)]
    xn_tm = at("xn_tm", [128, D], BF16, A0 + 49152)
    xn_fm = [at(f"xn_fm{i}", [128, NKC * 1024], BF16, A0 + 57344 + i * 65536) for i in range(2)]
    wsb = [at(f"wsb{i}", [128, 2048], BF16, A0 + 188416 + i * 4096) for i in range(3)]
    vtm = at("vtm", [128, NTC * 256], BF16, A0 + 200704)
    qkev = [at(f"qkev{i}", [128, 1024], BF16, A0 + 204800 + i * 2048) for i in range(2)]

    gsb = [at(f"gsb{i}", [128, D], BF16, A0 + i * 8192) for i in range(2)]
    usb = [at(f"usb{i}", [128, D], BF16, A0 + 16384 + i * 8192) for i in range(2)]
    t1_sb = at("t1_sb", [128, 1024], F32, A0 + 32768)
    t4_sb = at("t4_sb", [128, 1024], F32, A0 + 36864)
    hbuf = [at(f"hbuf{i}", [128, 1024], BF16, A0 + 40960 + i * 2048) for i in range(2)]
    xn4 = xn_fm[1]

    k_sb = at("k_sb", [128, 8 * 4096], BF16, A0)
    v_sb = at("v_sb", [128, 32 * 1024], BF16, A0 + 65536)
    q_sb = at("q_sb", [128, 8 * 1024], BF16, A0 + 131072)
    ssb = at("ssb", [128, 4096], F32, A0 + 147456)
    p_sb = at("p_sb", [128, 4096], BF16, A0 + 163840)
    pt_sb = at("pt_sb", [128, 32 * 128], BF16, A0 + 172032)
    av_sb = at("av_sb", [128, 1024], BF16, A0 + 180224)
    av_fm = at("av_fm", [128, 8 * 1024], BF16, A0 + 182272)

    wo_sb = at("wo_sb", [128, 8 * 4096], BF16, A0)
    x3 = [at(f"x3_{i}", [128, D], F32, A0 + 65536 + i * 16384) for i in range(2)]
    opb = [at(f"opb{i}", [128, 2048], F32, A0 + 98304 + i * 8192) for i in range(2)]
    bo_sb = at("bo_sb", [1, D], BF16, A0 + 114688)
    ones16 = at("ones16", [1, 128], BF16, A0 + 122880)

    acc = at("acc", [128, NTC * 2048], F32, A0)
    dwn5 = [at(f"dwn5_{i}", [128, 8 * 2048], BF16, A0 + 65536 + i * 32768) for i in range(2)]
    h5 = [at(f"h5_{i}", [128, 8 * 1024], BF16, A0 + 131072 + i * 16384) for i in range(2)]
    ds_halfs = [at(f"ds_half{i}", [128, 2048], F32, A0 + 163840 + i * 8192) for i in range(2)]
    opt5 = [at(f"opt5_{i}", [128, 2048], F32, A0 + 188928 + i * 8192) for i in range(2)]

    PS = nc.alloc_psum_tensor("PS", [128, 4096], F32)[:]
    PSb = PS.bitcast(BF16)

    def bank(b, n=1):
        return PS[:, b * 512:(b + n) * 512]

    def bankb(b, n=1):
        return PSb[:, b * 1024:(b + n) * 1024]

    bld = B(nc)
    ld = bld.sem("ld")
    st = bld.sem("st")
    wld = bld.sem("wld")
    pe = bld.sem("pe")
    dv = bld.sem("dv")
    ac = bld.sem("ac")
    po = bld.sem("po")
    W, R = bld.wait, bld.run

    R("sync", lambda e: e.dma_start(out=ident, in_=ident_in), ld, 16)
    R("sync", lambda e: e.dma_start(out=eps_sb, in_=eps_in), ld, 16)
    R("sync", lambda e: e.dma_start(out=bqkv_sb, in_=bqkv_r), ld, 16)
    R("sync", lambda e: e.dma_start(out=gs_sb, in_=gs_r), ld, 16)
    R("sync", lambda e: e.dma_start(out=us_sb, in_=us_r), ld, 16)
    init_ld = ld.n

    SL_ORDER = [1, 2, 3, 0]
    st1 = dict(xb_free_dv=[0, 0, 0], xb_free_ac=[0, 0, 0], wsb_free=[0, 0, 0], whalf=0,
               qkev_free=[0, 0], qkev_free_sem=[None, None], qk_ps_free=[0, 0],
               tp_ps_free=[0, 0], vtm_free=0, ln_pe_done=0,
               xnfm_free=[0, 0], xn_ready=[0, 0, 0, 0])

    def emit_ln_chain(i, tc):
        s = SL_ORDER[i]
        xbv = xbuf[tc % 3]
        W("sync", dv, st1["xb_free_dv"][tc % 3])
        W("sync", ac, st1["xb_free_ac"][tc % 3])
        n_x = R("sync", lambda e, o=xbv, r0=s * SC + tc * 128:
                e.dma_start(out=o, in_=xb[r0:r0 + 128, :]), ld, 16)
        W("vector", ld, n_x)
        n_d1 = R("vector", lambda e, x_=xbv: e.tensor_reduce(sl(S_ROW), x_, AX.X, ALU.add), dv)
        W("scalar", ld, n_x)
        W("scalar", pe, st1["ln_pe_done"])
        R("scalar", lambda e, x_=xbv: e.activation(xn_tm, x_, AF.Square, accum_out=sl(SSQ)), ac)
        W("scalar", dv, n_d1)
        W("scalar", ld, init_ld)
        R("scalar", lambda e: e.activation(sl(NEGM), sl(S_ROW), AF.Identity, scale=-1.0 / D), ac)
        n_a3 = R("scalar", lambda e: e.activation(sl(E2EPS), sl(SSQ), AF.Identity,
                                                  scale=1.0 / D, bias=eps_sb), ac)
        W("vector", ac, n_a3)
        n_d2 = R("vector", lambda e: e.tensor_scalar(sl(VAREPS), sl(NEGM), sl(NEGM), sl(E2EPS),
                                                     ALU.mult, ALU.subtract), dv)
        W("scalar", dv, n_d2)
        n_a4 = R("scalar", lambda e: e.activation(sl(STD), sl(VAREPS), AF.Sqrt, scale=-1.0), ac)
        W("vector", ac, n_a4)
        n_d3 = R("vector", lambda e: e.reciprocal(sl(RSTD), sl(STD)), dv)
        W("vector", dv, n_d3)
        n_d4 = R("vector", lambda e: e.tensor_scalar(sl(MMR), sl(NEGM), sl(RSTD), None, ALU.mult), dv)
        W("scalar", dv, n_d4)
        n_a5 = R("scalar", lambda e, x_=xbv: e.activation(xn_tm, x_, AF.Identity,
                                                          scale=sl(RSTD), bias=sl(MMR)), ac)
        st1["xb_free_dv"][tc % 3] = n_d1
        st1["xb_free_ac"][tc % 3] = n_a5
        return n_a5

    def emit_ln_transposes(i, tc, n_a5):
        par = i % 2
        xnb = xn_fm[par]
        W("tensor", ld, init_ld)
        for g in range(4):
            bnk = 4 + (g % 2)
            W("tensor", ac, n_a5)
            W("tensor", dv, st1["tp_ps_free"][g % 2])
            for j in range(8):
                kc = g * 8 + j
                R("tensor", lambda e, kc=kc, j=j, bnk=bnk: e.transpose(
                    bankb(bnk)[:, j * 128:(j + 1) * 128],
                    xn_tm[:, kc * 128:(kc + 1) * 128], ident),
                  pe if j == 7 else None)
            n_t = pe.n
            W("vector", pe, n_t)
            W("vector", pe, st1["xnfm_free"][par])
            for j in range(8):
                kc = g * 8 + j
                R("vector", lambda e, kc=kc, j=j, bnk=bnk, tc=tc, xnb=xnb: e.tensor_copy(
                    xnb[:, kc * 1024 + tc * 128: kc * 1024 + tc * 128 + 128],
                    bankb(bnk)[:, j * 128:(j + 1) * 128]),
                  dv if j == 7 else None)
            st1["tp_ps_free"][g % 2] = dv.n
        st1["ln_pe_done"] = pe.n

    def emit_p1b_fc(i, idx, fc):
        s = SL_ORDER[i]
        par = i % 2
        xnb = xn_fm[par]
        hA = st1["whalf"] % 3
        hB = (st1["whalf"] + 1) % 3
        st1["whalf"] += 2
        W("gpsimd", pe, st1["wsb_free"][hA])
        n_wA = R("gpsimd", lambda e, fc=fc, hA=hA: e.dma_start(
            out=wsb[hA], in_=wqkv_r[fc][:, 0:16, :].rearrange("p k m -> p (k m)"),
            max_dma_last_dim=8192), wld, 16)
        W("gpsimd", pe, st1["wsb_free"][hB])
        n_wB = R("gpsimd", lambda e, fc=fc, hB=hB: e.dma_start(
            out=wsb[hB], in_=wqkv_r[fc][:, 16:32, :].rearrange("p k m -> p (k m)"),
            max_dma_last_dim=8192), wld, 16)
        ping = idx % 2
        psv = bank(ping * 2, 2)
        W("tensor", wld, n_wA)
        W("tensor", dv, st1["xn_ready"][i])
        W("tensor", ac, st1["qk_ps_free"][ping])
        for span in range(2):
            tgt = psv[:, span * 512:(span + 1) * 512]
            for kc in range(NKC):
                if kc == 16 and span == 0:
                    W("tensor", wld, n_wB)
                mark = (kc == NKC - 1) or (span == 1 and kc == 15)
                R("tensor", lambda e, t=tgt, kc=kc, span=span, xnb=xnb, hA=hA, hB=hB: e.matmul(
                    t, wsb[hA if kc < 16 else hB][:, (kc % 16) * 128:(kc % 16 + 1) * 128],
                    xnb[:, kc * 1024 + span * 512: kc * 1024 + span * 512 + 512],
                    start=(kc == 0), stop=(kc == NKC - 1)),
                  pe if mark else None)
            if span == 1:
                st1["wsb_free"][hA] = pe.n - 1
        n_mm = pe.n
        st1["wsb_free"][hB] = n_mm
        ev = qkev[idx % 2]
        W("scalar", pe, n_mm)
        if st1["qkev_free_sem"][idx % 2] is not None:
            W("scalar", st1["qkev_free_sem"][idx % 2], st1["qkev_free"][idx % 2])
        n_ev = R("scalar", lambda e, ev=ev, psv=psv, fc=fc: e.activation(
            ev, psv, AF.Identity, bias=bqkv_sb[:, fc:fc + 1]), ac)
        st1["qk_ps_free"][ping] = n_ev
        if fc < 8:
            W("sync", ac, n_ev)
            n_s = R("sync", lambda e, fc=fc, ev=ev: e.dma_start(out=q_dr[fc], in_=ev), st, 16)
            st1["qkev_free"][idx % 2], st1["qkev_free_sem"][idx % 2] = n_s, st
        elif fc < 16:
            W("sync", ac, n_ev)
            n_s = R("sync", lambda e, fc=fc, ev=ev, s=s: e.dma_start(
                out=kvK[fc - 8][:, s * SC:(s + 1) * SC], in_=ev), st, 16)
            st1["qkev_free"][idx % 2], st1["qkev_free_sem"][idx % 2] = n_s, st
        else:
            ec = fc - 16
            eh, el = ec // 2, ec % 2
            for g in range(2):
                bnk = 4 + g
                W("tensor", ac, n_ev)
                W("tensor", dv, st1["tp_ps_free"][g])
                for j in range(4):
                    ts_ = g * 4 + j
                    R("tensor", lambda e, ev=ev, ts_=ts_, j=j, bnk=bnk: e.transpose(
                        bankb(bnk)[:, j * 128:(j + 1) * 128],
                        ev[:, ts_ * 128:(ts_ + 1) * 128], ident),
                      pe if j == 3 else None)
                n_t = pe.n
                W("vector", pe, n_t)
                W("vector", st, st1["vtm_free"])
                for j in range(4):
                    ts_ = g * 4 + j
                    R("vector", lambda e, ts_=ts_, j=j, el=el, bnk=bnk: e.tensor_copy(
                        vtm[:, ts_ * 256 + el * 128: ts_ * 256 + el * 128 + 128],
                        bankb(bnk)[:, j * 128:(j + 1) * 128]),
                      dv if j == 3 else None)
                st1["tp_ps_free"][g] = dv.n
            st1["qkev_free"][idx % 2], st1["qkev_free_sem"][idx % 2] = pe.n, pe
            if el == 1:
                n_vt = dv.n
                for ts_ in range(NTC):
                    W("sync", dv, n_vt)
                    R("sync", lambda e, s=s, ts_=ts_, eh=eh: e.dma_start(
                        out=kvV[s * NTC + ts_, eh],
                        in_=vtm[:, ts_ * 256:(ts_ + 1) * 256]), st, 16)
                st1["vtm_free"] = st.n

    for tc in range(NTC):
        emit_ln_transposes(0, tc, emit_ln_chain(0, tc))
    st1["xn_ready"][0] = dv.n
    for i in range(NSL):
        s = SL_ORDER[i]
        fcs = list(range(24)) if s == 0 else list(range(8, 24))
        st1["xnfm_free"][(i + 1) % 2] = pe.n
        ln_todo = list(range(NTC)) if i + 1 < NSL else []
        pending = []
        for idx, fc in enumerate(fcs):
            if ln_todo and idx % 2 == 0:
                tc = ln_todo.pop(0)
                pending.append((tc, emit_ln_chain(i + 1, tc)))
            emit_p1b_fc(i, idx, fc)
            if pending and idx % 2 == 1:
                tc, na5 = pending.pop(0)
                emit_ln_transposes(i + 1, tc, na5)
        while ln_todo:
            tc = ln_todo.pop(0)
            pending.append((tc, emit_ln_chain(i + 1, tc)))
        while pending:
            tc, na5 = pending.pop(0)
            emit_ln_transposes(i + 1, tc, na5)
        if i + 1 < NSL:
            st1["xn_ready"][i + 1] = dv.n

    p1_pe, p1_dv, p1_ac, p1_st = pe.n, dv.n, ac.n, st.n
    if phases < 2:
        return _finish(nc, bld, tap_list, st)

    own_ready = st1["xn_ready"][3]
    W("gpsimd", pe, p1_pe)
    W("gpsimd", dv, p1_dv)
    W("tensor", ac, p1_ac)
    W("tensor", dv, p1_dv)
    if phases >= 3:
        W("sync", pe, p1_pe)
        W("sync", dv, p1_dv)
        W("sync", ac, p1_ac)
        for kc in range(28):
            R("sync", lambda e, kc=kc: e.dma_start(
                out=v_sb[:, kc * 1024:(kc + 1) * 1024].rearrange("p (h x) -> p h x", h=4),
                in_=kvV[kc].rearrange("h p x -> p h x")), ld, 16)

    g_ps_free = [0, 0]
    u_ps_free = [0, 0]
    gu_free = [0, 0]
    h_free = [0, 0]
    last_h_dv = 0
    for fc in range(NFC):
        ping = fc % 2
        gv, uv = gsb[ping], usb[ping]
        W("gpsimd", pe, gu_free[ping])
        R("gpsimd", lambda e, gv=gv, fc=fc: e.dma_start(
            out=gv, in_=gate_r[fc].rearrange("p k m -> p (k m)"), max_dma_last_dim=8192), wld, 16)
        n_u = R("gpsimd", lambda e, uv=uv, fc=fc: e.dma_start(
            out=uv, in_=up_r[fc].rearrange("p k m -> p (k m)"), max_dma_last_dim=8192), wld, 16)
        psg = bank(ping * 2, 2)
        psu = bank(4 + ping * 2, 2)
        W("tensor", wld, n_u)
        W("tensor", dv, own_ready)
        W("tensor", ac, g_ps_free[ping])
        W("tensor", ac, u_ps_free[ping])
        for wv, psv in ((gv, psg), (uv, psu)):
            for span in range(2):
                tgt = psv[:, span * 512:(span + 1) * 512]
                for kc in range(NKC):
                    R("tensor", lambda e, t=tgt, wv=wv, kc=kc, span=span: e.matmul(
                        t, wv[:, kc * 128:(kc + 1) * 128],
                        xn4[:, kc * 1024 + span * 512: kc * 1024 + span * 512 + 512],
                        start=(kc == 0), stop=(kc == NKC - 1)),
                      pe if kc == NKC - 1 else None)
        n_mm = pe.n
        gu_free[ping] = n_mm
        W("scalar", pe, n_mm)
        W("scalar", dv, last_h_dv)
        n_t1 = R("scalar", lambda e, psg=psg, fc=fc: e.activation(
            t1_sb, psg, AF.Silu, scale=gs_sb[:, fc:fc + 1]), ac)
        n_t4 = R("scalar", lambda e, psu=psu, fc=fc: e.activation(
            t4_sb, psu, AF.Identity, scale=us_sb[:, fc:fc + 1]), ac)
        g_ps_free[ping] = n_t1
        u_ps_free[ping] = n_t4
        hv = hbuf[ping]
        W("vector", ac, n_t4)
        W("vector", st, h_free[ping])
        n_h = R("vector", lambda e, hv=hv: e.tensor_tensor(hv, t1_sb, t4_sb, ALU.mult), dv)
        last_h_dv = n_h
        W("sync", dv, n_h)
        R("sync", lambda e, hv=hv, fc=fc: e.dma_start(out=h_dr[fc], in_=hv), st, 16)
        h_free[ping] = st.n
    p4_pe, p4_dv, p4_ac, p4_st = pe.n, dv.n, ac.n, st.n
    if phases < 3:
        return _finish(nc, bld, tap_list, st)

    W("sync", pe, p4_pe)
    W("sync", ac, p4_ac)
    W("sync", dv, p4_dv)
    for dc in range(8):
        R("sync", lambda e, dc=dc: e.dma_start(
            out=k_sb[:, dc * 4096:(dc + 1) * 4096], in_=kvK[dc]), ld, 16)
    for kc in range(28, NKC):
        R("sync", lambda e, kc=kc: e.dma_start(
            out=v_sb[:, kc * 1024:(kc + 1) * 1024].rearrange("p (h x) -> p h x", h=4),
            in_=kvV[kc].rearrange("h p x -> p h x")), ld, 16)
    for dc in range(8):
        R("sync", lambda e, dc=dc: e.dma_start(
            out=q_sb[:, dc * 1024:(dc + 1) * 1024], in_=q_dr[dc]), ld, 16)
    kvq_ld = ld.n

    sc_ps_free = [0, 0]
    av_ps_free = [0, 0]
    pt2_free = [0, 0]
    exp_done = [0] * 8
    ptpe_done = [0] * 8
    ptsb_done = [0] * 8
    avsb_done = [0] * 8
    avTpe_done = [0] * 8
    avT_done = [0] * 8
    rinv_done = [0] * 8

    def p2_scores_mm(qc):
        W("tensor", ld, kvq_ld)
        W("tensor", ac, p4_ac)
        W("tensor", dv, p4_dv)
        for ks in range(8):
            ping = ks % 2
            psv = bank(ping)
            W("tensor", dv, sc_ps_free[ping])
            for dc in range(8):
                R("tensor", lambda e, psv=psv, dc=dc, qc=qc, ks=ks: e.matmul(
                    psv, q_sb[:, dc * 1024 + qc * 128: dc * 1024 + qc * 128 + 128],
                    k_sb[:, dc * 4096 + ks * 512: dc * 4096 + ks * 512 + 512],
                    start=(dc == 0), stop=(dc == 7)),
                  pe if dc == 7 else None)
            n_mm = pe.n
            W("vector", pe, n_mm)
            if qc > 0:
                W("vector", ac, exp_done[qc - 1])
            R("vector", lambda e, psv=psv, ks=ks: e.tensor_scalar(
                ssb[:, ks * 512:(ks + 1) * 512], psv, 1.0 / 32.0, None, ALU.mult), dv)
            sc_ps_free[ping] = dv.n

    def p2_softmax(qc):
        R("vector", lambda e: e.tensor_reduce(sl(NEGMAX), ssb, AX.X, ALU.max, negate=True), dv)
        n_nm = dv.n
        W("scalar", dv, n_nm)
        if qc > 0:
            W("scalar", pe, ptpe_done[qc - 1])
        R("scalar", lambda e: e.activation(
            p_sb, ssb, AF.Exp, bias=sl(NEGMAX), accum_out=sl(SUMEXP)), ac)
        exp_done[qc] = ac.n
        W("vector", ac, exp_done[qc])
        R("vector", lambda e: e.reciprocal(sl(RINV), sl(SUMEXP)), dv)
        rinv_done[qc] = dv.n

    def p2_pt_av(qc):
        for g in range(4):
            bnk = 4 + (g % 2)
            W("tensor", ac, exp_done[qc])
            W("tensor", dv, pt2_free[g % 2])
            for j in range(8):
                kc = g * 8 + j
                R("tensor", lambda e, kc=kc, j=j, bnk=bnk: e.transpose(
                    bankb(bnk)[:, j * 128:(j + 1) * 128],
                    p_sb[:, kc * 128:(kc + 1) * 128], ident),
                  pe if j == 7 else None)
            n_t = pe.n
            W("vector", pe, n_t)
            for j in range(8):
                kc = g * 8 + j
                R("vector", lambda e, kc=kc, j=j, bnk=bnk: e.tensor_copy(
                    pt_sb[:, kc * 128:(kc + 1) * 128],
                    bankb(bnk)[:, j * 128:(j + 1) * 128]),
                  dv if j == 7 else None)
            pt2_free[g % 2] = dv.n
        ptpe_done[qc] = pe.n
        ptsb_done[qc] = dv.n
        for es in range(2):
            psv = bank(2 + es)
            W("tensor", dv, ptsb_done[qc])
            W("tensor", ac, av_ps_free[es])
            for kc in range(NKC):
                R("tensor", lambda e, psv=psv, kc=kc, es=es: e.matmul(
                    psv, pt_sb[:, kc * 128:(kc + 1) * 128],
                    v_sb[:, kc * 1024 + es * 512: kc * 1024 + es * 512 + 512],
                    start=(kc == 0), stop=(kc == NKC - 1)),
                  pe if kc == NKC - 1 else None)
            n_av = pe.n
            W("scalar", pe, n_av)
            W("scalar", dv, rinv_done[qc])
            if qc > 0:
                W("scalar", pe, avTpe_done[qc - 1])
            R("scalar", lambda e, psv=psv, es=es: e.activation(
                av_sb[:, es * 512:(es + 1) * 512], psv, AF.Identity, scale=sl(RINV)), ac)
            av_ps_free[es] = ac.n
        avsb_done[qc] = ac.n
        W("tensor", ac, avsb_done[qc])
        W("tensor", dv, pt2_free[0])
        for ec in range(8):
            R("tensor", lambda e, ec=ec: e.transpose(
                bankb(4)[:, ec * 128:(ec + 1) * 128],
                av_sb[:, ec * 128:(ec + 1) * 128], ident),
              pe if ec == 7 else None)
        avTpe_done[qc] = pe.n
        W("vector", pe, avTpe_done[qc])
        for ec in range(8):
            R("vector", lambda e, ec=ec, qc=qc: e.tensor_copy(
                av_fm[:, ec * 1024 + qc * 128: ec * 1024 + qc * 128 + 128],
                bankb(4)[:, ec * 128:(ec + 1) * 128]),
              dv if ec == 7 else None)
        pt2_free[0] = dv.n
        avT_done[qc] = dv.n

    p2_scores_mm(0)
    p2_softmax(0)
    for qc in range(1, 8):
        p2_scores_mm(qc)
        if qc == 7 and phases >= 4:
            W("gpsimd", pe, pe.n)
            for ec in range(8):
                R("gpsimd", lambda e, ec=ec: e.dma_start(
                    out=wo_sb[:, ec * 4096:(ec + 1) * 4096], in_=wo_r[ec],
                    max_dma_last_dim=8192), wld, 16)
        p2_pt_av(qc - 1)
        p2_softmax(qc)
    p2_pt_av(7)
    p2_pe, p2_dv, p2_ac = pe.n, dv.n, ac.n
    if phases < 4:
        return _finish(nc, bld, tap_list, st)

    W("gpsimd", ac, p2_ac)
    R("gpsimd", lambda e: e.dma_start(out=bo_sb, in_=bo_r, max_dma_last_dim=8192), wld, 16)
    wo_ld = wld.n
    W("sync", pe, p2_pe)
    W("sync", ac, p2_ac)
    W("sync", dv, p2_dv)
    R("sync", lambda e: e.dma_start(out=ones16, in_=ones16_in), ld, 16)
    ones_ld = ld.n

    x3_free = [0, 0]
    op_free = [0, 0]
    wo_ps_free = [0, 0]
    x3_ld = [0, 0]
    for tc in range(NTC):
        xv = x3[tc % 2]
        W("sync", dv, x3_free[tc % 2])
        x3_ld[tc % 2] = R("sync", lambda e, xv=xv, tc=tc: e.dma_start(
            out=xv, in_=xb[tc * 128:(tc + 1) * 128, :]), ld, 16)
        for half in range(2):
            ping = (tc * 2 + half) % 2
            psv = bank(ping * 4, 4)
            W("tensor", wld, wo_ld)
            W("tensor", ld, ones_ld)
            W("tensor", dv, wo_ps_free[ping])
            W("tensor", dv, avT_done[7])
            W("tensor", ac, p2_ac)
            for sp in range(4):
                tgt = psv[:, sp * 512:(sp + 1) * 512]
                coff = half * 2048 + sp * 512
                for ec in range(8):
                    R("tensor", lambda e, t=tgt, ec=ec, tc=tc, coff=coff: e.matmul(
                        t, av_fm[:, ec * 1024 + tc * 128: ec * 1024 + tc * 128 + 128],
                        wo_sb[:, ec * 4096 + coff: ec * 4096 + coff + 512],
                        start=(ec == 0), stop=False))
                R("tensor", lambda e, t=tgt, coff=coff: e.matmul(
                    t, ones16, bo_sb[0:1, coff:coff + 512], start=False, stop=True),
                  pe if sp == 3 else None)
            n_mm = pe.n
            opv = opb[ping]
            W("vector", pe, n_mm)
            W("vector", ld, x3_ld[tc % 2])
            W("vector", st, op_free[ping])
            R("vector", lambda e, opv=opv, psv=psv, xv=xv, half=half: e.tensor_tensor(
                opv, psv, xv[:, half * 2048:(half + 1) * 2048], ALU.add), dv)
            wo_ps_free[ping] = dv.n
            n_op = dv.n
            W("sync", dv, n_op)
            R("sync", lambda e, opv=opv, tc=tc, half=half: e.dma_start(
                out=op_dr[tc][:, half * 2048:(half + 1) * 2048], in_=opv), st, 16)
            op_free[ping] = st.n
        x3_free[tc % 2] = dv.n
    p3_pe, p3_dv, p3_ac, p3_st = pe.n, dv.n, ac.n, st.n
    if phases < 5:
        return _finish(nc, bld, tap_list, st)

    W("sync", pe, p3_pe)
    W("sync", ac, p3_ac)
    W("sync", dv, p3_dv)
    for ch in range(2):
        R("sync", lambda e, ch=ch: e.dma_start(
            out=ds_halfs[ch], in_=ds_r[0][ch * 2048:(ch + 1) * 2048].partition_broadcast(128)),
          ld, 16)
    ds_ld = ld.n
    W("gpsimd", pe, p3_pe)
    W("gpsimd", dv, p3_dv)
    W("gpsimd", st, p3_st)
    W("tensor", ac, p3_ac)
    W("tensor", dv, p3_dv)

    NSUP = (NFC + 7) // 8
    dw_ps_free = [0, 0]
    h5_free = [0, 0]
    d5_free = [0, 0]
    h5_ld = [0, 0]
    d5_ld = [0, 0]
    opt_used = [0, 0]
    ot_st = [0, 0]

    def emit_final_tc(ch, tc, acc_done):
        if True:
            ping = tc % 2
            opv = opt5[ping]
            W("sync", dv, opt_used[ping])
            W("sync", st, ot_st[ping])
            n_o = R("sync", lambda e, opv=opv, tc=tc, ch=ch: e.dma_start(
                out=opv, in_=op_dr[tc][:, ch * 2048:(ch + 1) * 2048]), ld, 16)
            W("vector", dv, acc_done)
            W("vector", ld, max(n_o, ds_ld))
            W("vector", st, ot_st[ping])
            R("vector", lambda e, tc=tc, ch=ch: e.tensor_tensor(
                acc[:, tc * 2048:(tc + 1) * 2048],
                ds_halfs[ch], acc[:, tc * 2048:(tc + 1) * 2048], ALU.mult), None)
            n_f = R("vector", lambda e, opv=opv, tc=tc: e.tensor_tensor(
                opv, acc[:, tc * 2048:(tc + 1) * 2048], opv, ALU.add), dv)
            opt_used[ping] = n_f
            W("sync", dv, n_f)
            R("sync", lambda e, opv=opv, tc=tc, ch=ch: e.dma_start(
                out=out_dr[tc * 128:(tc + 1) * 128, ch * 2048:(ch + 1) * 2048], in_=opv), st, 16)
            ot_st[ping] = st.n

    acc_done_prev = 0
    for ch in range(2):
        for fs in range(NSUP):
            nf = min(8, NFC - fs * 8)
            ping = fs % 2
            W("sync", pe, h5_free[ping])
            W("sync", st, p4_st)
            for j in range(nf):
                h5_ld[ping] = R("sync", lambda e, j=j, fc=fs * 8 + j, ping=ping: e.dma_start(
                    out=h5[ping][:, j * 1024:(j + 1) * 1024], in_=h_dr[fc]), ld, 16)
            W("gpsimd", pe, d5_free[ping])
            for j in range(nf):
                d5_ld[ping] = R("gpsimd", lambda e, j=j, fc=fs * 8 + j, ping=ping, ch=ch:
                                e.dma_start(
                    out=dwn5[ping][:, j * 2048:(j + 1) * 2048],
                    in_=dwn_r[fc][:, ch * 2048:(ch + 1) * 2048],
                    max_dma_last_dim=8192), wld, 16)
            grp_first = True
            grp_last = True
            for tc in range(NTC):
                if ch == 1 and fs == 0:
                    emit_final_tc(0, tc, acc_done_prev)
                pping = tc % 2
                psv = bank(pping * 4, 4)
                W("tensor", ld, h5_ld[ping])
                W("tensor", wld, d5_ld[ping])
                if grp_first:
                    W("tensor", dv, dw_ps_free[pping])
                for j in range(nf):
                    for sp in range(4):
                        R("tensor", lambda e, psv=psv, j=j, sp=sp, tc=tc, ping=ping,
                          s0=(grp_first and j == 0), s1=(grp_last and j == nf - 1): e.matmul(
                            psv[:, sp * 512:(sp + 1) * 512],
                            h5[ping][:, j * 1024 + tc * 128: j * 1024 + tc * 128 + 128],
                            dwn5[ping][:, j * 2048 + sp * 512: j * 2048 + sp * 512 + 512],
                            start=s0, stop=s1),
                          pe if (j == nf - 1 and sp == 3) else None)
                n_mm = pe.n
                if grp_last:
                    W("vector", pe, n_mm)
                    if fs == 0:
                        W("vector", st, p4_st)
                        R("vector", lambda e, psv=psv, tc=tc: e.tensor_copy(
                            acc[:, tc * 2048:(tc + 1) * 2048], psv), dv)
                    else:
                        R("vector", lambda e, psv=psv, tc=tc: e.tensor_tensor(
                            acc[:, tc * 2048:(tc + 1) * 2048], psv,
                            acc[:, tc * 2048:(tc + 1) * 2048], ALU.add), dv)
                    dw_ps_free[pping] = dv.n
            h5_free[ping] = pe.n
            d5_free[ping] = pe.n
        acc_done_prev = dv.n
        if ch == 0 and acc_tap is not None:
            W("sync", dv, acc_done_prev)
            R("sync", lambda e: e.dma_start(out=acc_tap, in_=acc), st, 16)
            R("sync", lambda e: e.dma_start(out=ds_tap, in_=ds_halfs[0]), st, 16)
            W("sync", st, st.n)
    for tc in range(NTC):
        emit_final_tc(1, tc, acc_done_prev)
    return _finish(nc, bld, tap_list, st)


def _finish(nc, bld, tap_list, st):
    for t, src in tap_list:
        bld.wait("sync", st, st.n)
        bld.run("sync", lambda e, t=t, src=src: e.dma_start(out=t, in_=src), st, 16)
    bld.wait("sync", st, st.n)

    from contextlib import ExitStack
    with ExitStack() as es:
        for s in bld.sems.values():
            s.h = es.enter_context(nc.semaphore(s.name))
        block = es.enter_context(nc.Block())
        for name in ENGINES:
            def body(e, name=name):
                for f in bld.prog[name]:
                    f(e)
            getattr(block, name)(body)
    return nc


def _relayout(Wqkv, bqkv, Wo, bo, gate_w, gate_s, up_w, up_s, down_w, down_s):
    d = {}
    d["wqkv_r"] = np.ascontiguousarray(Wqkv.reshape(NKC, 128, 24, 128).transpose(2, 1, 0, 3))
    d["bqkv_r"] = np.ascontiguousarray(bqkv.reshape(24, 128).T)
    d["wo_r"] = np.ascontiguousarray(Wo.reshape(8, 128, D))
    d["bo_r"] = np.ascontiguousarray(bo.reshape(1, D))
    d["gate_r"] = np.ascontiguousarray(gate_w.reshape(NFC, 128, NKC, 128).transpose(0, 3, 2, 1))
    d["up_r"] = np.ascontiguousarray(up_w.reshape(NFC, 128, NKC, 128).transpose(0, 3, 2, 1))
    d["gs_r"] = np.ascontiguousarray(gate_s.reshape(NFC, 128).T)
    d["us_r"] = np.ascontiguousarray(up_s.reshape(NFC, 128).T)
    d["dwn_r"] = np.ascontiguousarray(down_w.T.reshape(NFC, 128, D))
    d["ds_r"] = np.ascontiguousarray(down_s.reshape(1, D))
    d["ident_in"] = np.eye(128, dtype=ml_dtypes.bfloat16)
    d["eps_in"] = np.full((128, 1), 1e-5, np.float32)
    d["ones16_in"] = np.ones((1, 128), ml_dtypes.bfloat16)
    return d


def kernel(L0_attn_x, L0_attn_Wqkv, L0_attn_bqkv, L0_attn_Wo, L0_attn_bo,
           L0_ffn_gate_w, L0_ffn_gate_s, L0_ffn_up_w, L0_ffn_up_s,
           L0_ffn_down_w, L0_ffn_down_s):
    x = np.asarray(L0_attn_x, dtype=np.float32)
    shared = _relayout(np.asarray(L0_attn_Wqkv), np.asarray(L0_attn_bqkv),
                       np.asarray(L0_attn_Wo), np.asarray(L0_attn_bo),
                       np.asarray(L0_ffn_gate_w), np.asarray(L0_ffn_gate_s),
                       np.asarray(L0_ffn_up_w), np.asarray(L0_ffn_up_s),
                       np.asarray(L0_ffn_down_w), np.asarray(L0_ffn_down_s))
    nc = build_nc()
    in_maps = []
    for c in range(8):
        b, rot = c // 4, c % 4
        xr = np.concatenate([x[b, rot * SC:], x[b, :rot * SC]], axis=0)
        im = dict(shared)
        im["xb"] = np.ascontiguousarray(xr)
        in_maps.append(im)
    res = run_bass_kernel_spmd(nc, in_maps, core_ids=list(range(8)))
    out = np.empty((2, NSL * SC, D), np.float32)
    for c in range(8):
        b, rot = c // 4, c % 4
        out[b, rot * SC:(rot + 1) * SC] = np.asarray(res.results[c]["out"])
    return out


# version 11 (final)
# speedup vs baseline: 1.0054x; 1.0054x over previous
import numpy as np
import ml_dtypes
import concourse.bass as bass
import concourse.mybir as mybir
from concourse.bass_utils import run_bass_kernel_spmd

F32 = mybir.dt.float32
BF16 = mybir.dt.bfloat16
I32 = mybir.dt.int32
AF = mybir.ActivationFunctionType
ALU = mybir.AluOpType
AX = mybir.AxisListType

D = 4096
DFF = 11008
NFC = DFF // 128
NKC = D // 128
SC = 1024
NTC = SC // 128
NSL = 4
ENGINES = ("tensor", "vector", "scalar", "gpsimd", "sync")


class Sem:
    __slots__ = ("name", "h", "n")

    def __init__(self, name):
        self.name = name
        self.h = None
        self.n = 0


class B:
    def __init__(self, nc):
        self.nc = nc
        self.prog = {e: [] for e in ENGINES}
        self.sems = {}
        self.waited = {e: {} for e in ENGINES}

    def sem(self, name):
        s = Sem(name)
        self.sems[name] = s
        return s

    def wait(self, eng, sem, val=None):
        if val is None:
            val = sem.n
        if val <= 0:
            return
        if self.waited[eng].get(sem.name, 0) >= val:
            return
        self.waited[eng][sem.name] = val
        self.prog[eng].append(lambda e, s=sem, v=val: e.wait_ge(s.h, v))

    def run(self, eng, fn, inc=None, k=1):
        if inc is None:
            self.prog[eng].append(fn)
            return None
        inc.n += k

        def f(e, fn=fn, s=inc, k=k):
            fn(e).then_inc(s.h, k)
        self.prog[eng].append(f)
        return inc.n


def build_nc(phases=5, taps=False):
    nc = bass.Bass("TRN2", target_bir_lowering=False)

    xb = nc.dram_tensor("xb", [NSL * SC, D], F32, kind="ExternalInput").ap()
    wqkv_r = nc.dram_tensor("wqkv_r", [24, 128, NKC, 128], F32, kind="ExternalInput").ap()
    bqkv_r = nc.dram_tensor("bqkv_r", [128, 24], F32, kind="ExternalInput").ap()
    wo_r = nc.dram_tensor("wo_r", [8, 128, D], F32, kind="ExternalInput").ap()
    bo_r = nc.dram_tensor("bo_r", [1, D], F32, kind="ExternalInput").ap()
    gate_r = nc.dram_tensor("gate_r", [NFC, 128, NKC, 128], I32, kind="ExternalInput").ap()
    up_r = nc.dram_tensor("up_r", [NFC, 128, NKC, 128], I32, kind="ExternalInput").ap()
    gs_r = nc.dram_tensor("gs_r", [128, NFC], F32, kind="ExternalInput").ap()
    us_r = nc.dram_tensor("us_r", [128, NFC], F32, kind="ExternalInput").ap()
    dwn_r = nc.dram_tensor("dwn_r", [NFC, 128, D], I32, kind="ExternalInput").ap()
    ds_r = nc.dram_tensor("ds_r", [1, D], F32, kind="ExternalInput").ap()
    ident_in = nc.dram_tensor("ident_in", [128, 128], BF16, kind="ExternalInput").ap()
    eps_in = nc.dram_tensor("eps_in", [128, 1], F32, kind="ExternalInput").ap()
    ones16_in = nc.dram_tensor("ones16_in", [1, 128], BF16, kind="ExternalInput").ap()
    out_dr = nc.dram_tensor("out", [SC, D], F32, kind="ExternalOutput").ap()

    kvK = nc.dram_tensor("kvK", [8, 128, NSL * SC], BF16).ap()
    kvV = nc.dram_tensor("kvV", [NSL * NTC, 4, 128, 256], BF16).ap()
    q_dr = nc.dram_tensor("q_dr", [8, 128, SC], BF16).ap()
    op_dr = nc.dram_tensor("op_dr", [NTC, 128, D], F32).ap()
    h_dr = nc.dram_tensor("h_dr", [NFC, 128, SC], BF16).ap()

    tap_list = []
    if taps:
        def tap(name, src_ap, shape, dt):
            t = nc.dram_tensor(name, shape, dt, kind="ExternalOutput").ap()
            tap_list.append((t, src_ap))
        if phases >= 1:
            tap("tap_q", q_dr, [8, 128, SC], BF16)
            tap("tap_k", kvK, [8, 128, NSL * SC], BF16)
            tap("tap_v", kvV, [NSL * NTC, 4, 128, 256], BF16)
        if phases >= 2:
            tap("tap_h", h_dr, [NFC, 128, SC], BF16)
        if phases >= 4:
            tap("tap_op", op_dr, [NTC, 128, D], F32)
    acc_tap = nc.dram_tensor("tap_acc", [128, NTC * 2048], F32, kind="ExternalOutput").ap() if taps and phases >= 5 else None
    ds_tap = nc.dram_tensor("tap_ds", [128, 2048], F32, kind="ExternalOutput").ap() if taps and phases >= 5 else None

    BASE = 16512
    TOP = 229344

    def at(name, shape, dt, off):
        nbytes = int(np.prod(shape[1:])) * mybir.dt.size(dt)
        assert BASE <= off and off + nbytes <= TOP, (name, off, off + nbytes)
        return nc.alloc_sbuf_tensor_at(name, shape, dt, offset=off)[:]

    ident = at("ident", [128, 128], BF16, BASE)
    eps_sb = at("eps_sb", [128, 1], F32, BASE + 256)
    bqkv_sb = at("bqkv_sb", [128, 24], F32, BASE + 288)
    gs_sb = at("gs_sb", [128, NFC], F32, BASE + 384)
    us_sb = at("us_sb", [128, NFC], F32, BASE + 736)
    smalls = at("smalls", [128, 24], F32, BASE + 1088)
    A0 = BASE + 1184

    def sl(i):
        return smalls[:, i:i + 1]

    S_ROW, SSQ, NEGM, E2EPS, MSQ, VAREPS, STD, RSTD, MMR, NEGMAX, SUMEXP, RINV = range(12)

    def slp(tc, i):
        return smalls[:, (tc % 2) * 12 + i:(tc % 2) * 12 + i + 1]

    xbuf = [at(f"xbuf{i}", [128, D], F32, A0 + i * 16384) for i in range(3)]
    xn_tm = at("xn_tm", [128, D], BF16, A0 + 49152)
    xn_fm = [at(f"xn_fm{i}", [128, NKC * 1024], BF16, A0 + 57344 + i * 65536) for i in range(2)]
    wsb = [at(f"wsb{i}", [128, 2048], BF16, A0 + 188416 + i * 4096) for i in range(3)]
    vtm = at("vtm", [128, NTC * 256], BF16, A0 + 200704)
    qkev = [at(f"qkev{i}", [128, 1024], BF16, A0 + 204800 + i * 2048) for i in range(2)]

    gsb = [at(f"gsb{i}", [128, D], BF16, A0 + i * 8192) for i in range(2)]
    usb = [at(f"usb{i}", [128, D], BF16, A0 + 16384 + i * 8192) for i in range(2)]
    t1_sb = at("t1_sb", [128, 1024], F32, A0 + 196608)
    t4_sb = at("t4_sb", [128, 1024], F32, A0 + 200704)
    hbuf = [at(f"hbuf{i}", [128, 1024], BF16, A0 + 204800 + i * 2048) for i in range(2)]
    xn4 = xn_fm[1]

    k_sb = at("k_sb", [128, 8 * 4096], BF16, A0)
    v_sb = at("v_sb", [128, 32 * 1024], BF16, A0 + 65536)
    q_sb = at("q_sb", [128, 8 * 1024], BF16, A0 + 131072)
    ssb = at("ssb", [128, 4096], F32, A0 + 147456)
    p_sb = at("p_sb", [128, 4096], BF16, A0 + 163840)
    pt_sb = at("pt_sb", [128, 32 * 128], BF16, A0 + 172032)
    av_sb = at("av_sb", [128, 1024], BF16, A0 + 196608)
    av_fm = at("av_fm", [128, 8 * 1024], BF16, A0 + 180224)

    wo_sb = at("wo_sb", [128, 8 * 4096], BF16, A0)
    x3 = [at(f"x3_{i}", [128, D], F32, A0 + 131072 + i * 16384) for i in range(2)]
    opb = [at(f"opb{i}", [128, 2048], F32, A0 + 163840 + i * 8192) for i in range(2)]
    bo_sb = at("bo_sb", [1, D], BF16, A0 + 198656)
    ones16 = at("ones16", [1, 128], BF16, A0 + 206848)

    acc = at("acc", [128, NTC * 2048], F32, A0)
    dwn5 = [at(f"dwn5_{i}", [128, 8 * 2048], BF16, A0 + 65536 + i * 32768) for i in range(2)]
    h5 = [at(f"h5_{i}", [128, 8 * 1024], BF16, A0 + 131072 + i * 16384) for i in range(2)]
    ds_halfs = [at(f"ds_half{i}", [128, 2048], F32, A0 + 163840 + i * 8192) for i in range(2)]
    opt5 = [at(f"opt5_{i}", [128, 2048], F32, A0 + 188928 + i * 8192) for i in range(2)]

    PS = nc.alloc_psum_tensor("PS", [128, 4096], F32)[:]
    PSb = PS.bitcast(BF16)

    def bank(b, n=1):
        return PS[:, b * 512:(b + n) * 512]

    def bankb(b, n=1):
        return PSb[:, b * 1024:(b + n) * 1024]

    bld = B(nc)
    ld = bld.sem("ld")
    st = bld.sem("st")
    wld = bld.sem("wld")
    pe = bld.sem("pe")
    dv = bld.sem("dv")
    ac = bld.sem("ac")
    po = bld.sem("po")
    W, R = bld.wait, bld.run

    R("sync", lambda e: e.dma_start(out=ident, in_=ident_in), ld, 16)
    R("sync", lambda e: e.dma_start(out=eps_sb, in_=eps_in), ld, 16)
    R("sync", lambda e: e.dma_start(out=bqkv_sb, in_=bqkv_r), ld, 16)
    R("sync", lambda e: e.dma_start(out=gs_sb, in_=gs_r), ld, 16)
    R("sync", lambda e: e.dma_start(out=us_sb, in_=us_r), ld, 16)
    init_ld = ld.n

    SL_ORDER = [1, 2, 3, 0]
    st1 = dict(xb_free_dv=[0, 0, 0], xb_free_ac=[0, 0, 0], wsb_free=[0, 0, 0], whalf=0,
               qkev_free=[0, 0], qkev_free_sem=[None, None], qk_ps_free=[0, 0],
               tp_ps_free=[0, 0], vtm_free=0, ln_pe_done=0,
               xnfm_free=[0, 0], xn_ready=[0, 0, 0, 0])

    def emit_ln_A(i, tc):
        s = SL_ORDER[i]
        xbv = xbuf[tc % 3]
        W("sync", dv, st1["xb_free_dv"][tc % 3])
        W("sync", ac, st1["xb_free_ac"][tc % 3])
        n_x = R("sync", lambda e, o=xbv, r0=s * SC + tc * 128:
                e.dma_start(out=o, in_=xb[r0:r0 + 128, :]), ld, 16)
        W("vector", ld, n_x)
        n_d1 = R("vector", lambda e, x_=xbv, tc=tc: e.tensor_reduce(
            slp(tc, S_ROW), x_, AX.X, ALU.add), dv)
        W("scalar", ld, n_x)
        W("scalar", pe, st1["ln_pe_done"])
        R("scalar", lambda e, x_=xbv, tc=tc: e.activation(
            xn_tm, x_, AF.Square, accum_out=slp(tc, SSQ)), ac)
        return n_d1

    def emit_ln_B(i, tc, n_d1):
        xbv = xbuf[tc % 3]
        W("scalar", dv, n_d1)
        W("scalar", ld, init_ld)
        R("scalar", lambda e, tc=tc: e.activation(
            slp(tc, NEGM), slp(tc, S_ROW), AF.Identity, scale=-1.0 / D), ac)
        n_a3 = R("scalar", lambda e, tc=tc: e.activation(
            slp(tc, E2EPS), slp(tc, SSQ), AF.Identity, scale=1.0 / D, bias=eps_sb), ac)
        W("vector", ac, n_a3)
        n_d2 = R("vector", lambda e, tc=tc: e.tensor_scalar(
            slp(tc, VAREPS), slp(tc, NEGM), slp(tc, NEGM), slp(tc, E2EPS),
            ALU.mult, ALU.subtract), dv)
        W("scalar", dv, n_d2)
        n_a4 = R("scalar", lambda e, tc=tc: e.activation(
            slp(tc, STD), slp(tc, VAREPS), AF.Sqrt, scale=-1.0), ac)
        W("vector", ac, n_a4)
        n_d3 = R("vector", lambda e, tc=tc: e.reciprocal(slp(tc, RSTD), slp(tc, STD)), dv)
        W("vector", dv, n_d3)
        n_d4 = R("vector", lambda e, tc=tc: e.tensor_scalar(
            slp(tc, MMR), slp(tc, NEGM), slp(tc, RSTD), None, ALU.mult), dv)
        W("scalar", dv, n_d4)
        n_a5 = R("scalar", lambda e, x_=xbv, tc=tc: e.activation(
            xn_tm, x_, AF.Identity, scale=slp(tc, RSTD), bias=slp(tc, MMR)), ac)
        st1["xb_free_dv"][tc % 3] = n_d1
        st1["xb_free_ac"][tc % 3] = n_a5
        return n_a5

    def emit_ln_chain(i, tc):
        return emit_ln_B(i, tc, emit_ln_A(i, tc))

    def emit_ln_transposes(i, tc, n_a5):
        par = i % 2
        xnb = xn_fm[par]
        W("tensor", ld, init_ld)
        for g in range(4):
            bnk = 4 + (g % 2)
            W("tensor", ac, n_a5)
            W("tensor", dv, st1["tp_ps_free"][g % 2])
            for j in range(8):
                kc = g * 8 + j
                R("tensor", lambda e, kc=kc, j=j, bnk=bnk: e.transpose(
                    bankb(bnk)[:, j * 128:(j + 1) * 128],
                    xn_tm[:, kc * 128:(kc + 1) * 128], ident),
                  pe if j == 7 else None)
            n_t = pe.n
            W("vector", pe, n_t)
            W("vector", pe, st1["xnfm_free"][par])
            for j in range(8):
                kc = g * 8 + j
                R("vector", lambda e, kc=kc, j=j, bnk=bnk, tc=tc, xnb=xnb: e.tensor_copy(
                    xnb[:, kc * 1024 + tc * 128: kc * 1024 + tc * 128 + 128],
                    bankb(bnk)[:, j * 128:(j + 1) * 128]),
                  dv if j == 7 else None)
            st1["tp_ps_free"][g % 2] = dv.n
        st1["ln_pe_done"] = pe.n

    def emit_p1b_fc(i, idx, fc):
        s = SL_ORDER[i]
        par = i % 2
        xnb = xn_fm[par]
        hA = st1["whalf"] % 3
        hB = (st1["whalf"] + 1) % 3
        st1["whalf"] += 2
        W("gpsimd", pe, st1["wsb_free"][hA])
        n_wA = R("gpsimd", lambda e, fc=fc, hA=hA: e.dma_start(
            out=wsb[hA], in_=wqkv_r[fc][:, 0:16, :].rearrange("p k m -> p (k m)"),
            max_dma_last_dim=8192), wld, 16)
        W("gpsimd", pe, st1["wsb_free"][hB])
        n_wB = R("gpsimd", lambda e, fc=fc, hB=hB: e.dma_start(
            out=wsb[hB], in_=wqkv_r[fc][:, 16:32, :].rearrange("p k m -> p (k m)"),
            max_dma_last_dim=8192), wld, 16)
        ping = idx % 2
        psv = bank(ping * 2, 2)
        W("tensor", wld, n_wA)
        W("tensor", dv, st1["xn_ready"][i])
        W("tensor", ac, st1["qk_ps_free"][ping])
        for span in range(2):
            tgt = psv[:, span * 512:(span + 1) * 512]
            for kc in range(NKC):
                if kc == 16 and span == 0:
                    W("tensor", wld, n_wB)
                mark = (kc == NKC - 1) or (span == 1 and kc == 15)
                R("tensor", lambda e, t=tgt, kc=kc, span=span, xnb=xnb, hA=hA, hB=hB: e.matmul(
                    t, wsb[hA if kc < 16 else hB][:, (kc % 16) * 128:(kc % 16 + 1) * 128],
                    xnb[:, kc * 1024 + span * 512: kc * 1024 + span * 512 + 512],
                    start=(kc == 0), stop=(kc == NKC - 1)),
                  pe if mark else None)
            if span == 1:
                st1["wsb_free"][hA] = pe.n - 1
        n_mm = pe.n
        st1["wsb_free"][hB] = n_mm
        ev = qkev[idx % 2]
        W("scalar", pe, n_mm)
        if st1["qkev_free_sem"][idx % 2] is not None:
            W("scalar", st1["qkev_free_sem"][idx % 2], st1["qkev_free"][idx % 2])
        n_ev = R("scalar", lambda e, ev=ev, psv=psv, fc=fc: e.activation(
            ev, psv, AF.Identity, bias=bqkv_sb[:, fc:fc + 1]), ac)
        st1["qk_ps_free"][ping] = n_ev
        if fc < 8:
            W("sync", ac, n_ev)
            n_s = R("sync", lambda e, fc=fc, ev=ev: e.dma_start(out=q_dr[fc], in_=ev), st, 16)
            st1["qkev_free"][idx % 2], st1["qkev_free_sem"][idx % 2] = n_s, st
        elif fc < 16:
            W("sync", ac, n_ev)
            n_s = R("sync", lambda e, fc=fc, ev=ev, s=s: e.dma_start(
                out=kvK[fc - 8][:, s * SC:(s + 1) * SC], in_=ev), st, 16)
            st1["qkev_free"][idx % 2], st1["qkev_free_sem"][idx % 2] = n_s, st
        else:
            ec = fc - 16
            eh, el = ec // 2, ec % 2
            for g in range(2):
                bnk = 4 + g
                W("tensor", ac, n_ev)
                W("tensor", dv, st1["tp_ps_free"][g])
                for j in range(4):
                    ts_ = g * 4 + j
                    R("tensor", lambda e, ev=ev, ts_=ts_, j=j, bnk=bnk: e.transpose(
                        bankb(bnk)[:, j * 128:(j + 1) * 128],
                        ev[:, ts_ * 128:(ts_ + 1) * 128], ident),
                      pe if j == 3 else None)
                n_t = pe.n
                W("vector", pe, n_t)
                W("vector", st, st1["vtm_free"])
                for j in range(4):
                    ts_ = g * 4 + j
                    R("vector", lambda e, ts_=ts_, j=j, el=el, bnk=bnk: e.tensor_copy(
                        vtm[:, ts_ * 256 + el * 128: ts_ * 256 + el * 128 + 128],
                        bankb(bnk)[:, j * 128:(j + 1) * 128]),
                      dv if j == 3 else None)
                st1["tp_ps_free"][g] = dv.n
            st1["qkev_free"][idx % 2], st1["qkev_free_sem"][idx % 2] = pe.n, pe
            if el == 1:
                n_vt = dv.n
                for ts_ in range(NTC):
                    W("sync", dv, n_vt)
                    R("sync", lambda e, s=s, ts_=ts_, eh=eh: e.dma_start(
                        out=kvV[s * NTC + ts_, eh],
                        in_=vtm[:, ts_ * 256:(ts_ + 1) * 256]), st, 16)
                st1["vtm_free"] = st.n

    nda = {0: emit_ln_A(0, 0)}
    for tc in range(NTC):
        if tc + 1 < NTC:
            nda[tc + 1] = emit_ln_A(0, tc + 1)
        na5 = emit_ln_B(0, tc, nda[tc])
        emit_ln_transposes(0, tc, na5)
    st1["xn_ready"][0] = dv.n
    for i in range(NSL):
        s = SL_ORDER[i]
        fcs = list(range(24)) if s == 0 else list(range(8, 24))
        st1["xnfm_free"][(i + 1) % 2] = pe.n
        ln_todo = list(range(NTC)) if i + 1 < NSL else []
        pending = []
        for idx, fc in enumerate(fcs):
            if ln_todo and idx % 2 == 0:
                tc = ln_todo.pop(0)
                pending.append((tc, emit_ln_chain(i + 1, tc)))
            emit_p1b_fc(i, idx, fc)
            if pending and idx % 2 == 1:
                tc, na5 = pending.pop(0)
                emit_ln_transposes(i + 1, tc, na5)
        while ln_todo:
            tc = ln_todo.pop(0)
            pending.append((tc, emit_ln_chain(i + 1, tc)))
        while pending:
            tc, na5 = pending.pop(0)
            emit_ln_transposes(i + 1, tc, na5)
        if i + 1 < NSL:
            st1["xn_ready"][i + 1] = dv.n

    p1_pe, p1_dv, p1_ac, p1_st = pe.n, dv.n, ac.n, st.n
    if phases < 2:
        return _finish(nc, bld, tap_list, st)

    own_ready = st1["xn_ready"][3]
    W("gpsimd", pe, p1_pe)
    W("gpsimd", dv, p1_dv)
    W("tensor", ac, p1_ac)
    W("tensor", dv, p1_dv)
    W("scalar", st, p1_st)
    if phases >= 3:
        W("sync", pe, p1_pe)
        W("sync", dv, p1_dv)
        W("sync", ac, p1_ac)
        W("sync", st, p1_st)
        for dc in range(4, 8):
            R("sync", lambda e, dc=dc: e.dma_start(
                out=k_sb[:, dc * 4096:(dc + 1) * 4096], in_=kvK[dc]), ld, 16)
        for kc in range(28):
            R("sync", lambda e, kc=kc: e.dma_start(
                out=v_sb[:, kc * 1024:(kc + 1) * 1024].rearrange("p (h x) -> p h x", h=4),
                in_=kvV[kc].rearrange("h p x -> p h x")), ld, 16)

    g_ps_free = [0, 0]
    u_ps_free = [0, 0]
    gu_free = [0, 0]
    h_free = [0, 0]
    last_h_dv = 0
    for fc in range(NFC):
        ping = fc % 2
        gv, uv = gsb[ping], usb[ping]
        W("gpsimd", pe, gu_free[ping])
        R("gpsimd", lambda e, gv=gv, fc=fc: e.dma_start(
            out=gv, in_=gate_r[fc].rearrange("p k m -> p (k m)"), max_dma_last_dim=8192), wld, 16)
        n_u = R("gpsimd", lambda e, uv=uv, fc=fc: e.dma_start(
            out=uv, in_=up_r[fc].rearrange("p k m -> p (k m)"), max_dma_last_dim=8192), wld, 16)
        psg = bank(ping * 2, 2)
        psu = bank(4 + ping * 2, 2)
        W("tensor", wld, n_u)
        W("tensor", dv, own_ready)
        W("tensor", ac, g_ps_free[ping])
        W("tensor", ac, u_ps_free[ping])
        for wv, psv in ((gv, psg), (uv, psu)):
            for span in range(2):
                tgt = psv[:, span * 512:(span + 1) * 512]
                for kc in range(NKC):
                    R("tensor", lambda e, t=tgt, wv=wv, kc=kc, span=span: e.matmul(
                        t, wv[:, kc * 128:(kc + 1) * 128],
                        xn4[:, kc * 1024 + span * 512: kc * 1024 + span * 512 + 512],
                        start=(kc == 0), stop=(kc == NKC - 1)),
                      pe if kc == NKC - 1 else None)
        n_mm = pe.n
        gu_free[ping] = n_mm
        W("scalar", pe, n_mm)
        W("scalar", dv, last_h_dv)
        n_t1 = R("scalar", lambda e, psg=psg, fc=fc: e.activation(
            t1_sb, psg, AF.Silu, scale=gs_sb[:, fc:fc + 1]), ac)
        n_t4 = R("scalar", lambda e, psu=psu, fc=fc: e.activation(
            t4_sb, psu, AF.Identity, scale=us_sb[:, fc:fc + 1]), ac)
        g_ps_free[ping] = n_t1
        u_ps_free[ping] = n_t4
        hv = hbuf[ping]
        W("vector", ac, n_t4)
        W("vector", st, h_free[ping])
        n_h = R("vector", lambda e, hv=hv: e.tensor_tensor(hv, t1_sb, t4_sb, ALU.mult), dv)
        last_h_dv = n_h
        W("sync", dv, n_h)
        R("sync", lambda e, hv=hv, fc=fc: e.dma_start(out=h_dr[fc], in_=hv), st, 16)
        h_free[ping] = st.n
    p4_pe, p4_dv, p4_ac, p4_st = pe.n, dv.n, ac.n, st.n
    if phases < 3:
        return _finish(nc, bld, tap_list, st)

    W("sync", pe, p4_pe)
    W("sync", ac, p4_ac)
    W("sync", dv, p4_dv)
    for dc in range(4):
        R("sync", lambda e, dc=dc: e.dma_start(
            out=k_sb[:, dc * 4096:(dc + 1) * 4096], in_=kvK[dc]), ld, 16)
    for kc in range(28, NKC):
        R("sync", lambda e, kc=kc: e.dma_start(
            out=v_sb[:, kc * 1024:(kc + 1) * 1024].rearrange("p (h x) -> p h x", h=4),
            in_=kvV[kc].rearrange("h p x -> p h x")), ld, 16)
    for dc in range(8):
        R("sync", lambda e, dc=dc: e.dma_start(
            out=q_sb[:, dc * 1024:(dc + 1) * 1024], in_=q_dr[dc]), ld, 16)
    kvq_ld = ld.n

    sc_ps_free = [0, 0]
    av_ps_free = [0, 0]
    pt2_free = [0, 0]
    exp_done = [0] * 8
    ptpe_done = [0] * 8
    ptsb_done = [0] * 8
    avsb_done = [0] * 8
    avTpe_done = [0] * 8
    avT_done = [0] * 8
    rinv_done = [0] * 8

    def p2_scores_mm(qc):
        W("tensor", ld, kvq_ld)
        W("tensor", ac, p4_ac)
        W("tensor", dv, p4_dv)
        for ks in range(8):
            ping = ks % 2
            psv = bank(ping)
            W("tensor", dv, sc_ps_free[ping])
            for dc in range(8):
                R("tensor", lambda e, psv=psv, dc=dc, qc=qc, ks=ks: e.matmul(
                    psv, q_sb[:, dc * 1024 + qc * 128: dc * 1024 + qc * 128 + 128],
                    k_sb[:, dc * 4096 + ks * 512: dc * 4096 + ks * 512 + 512],
                    start=(dc == 0), stop=(dc == 7)),
                  pe if dc == 7 else None)
            n_mm = pe.n
            W("vector", pe, n_mm)
            if qc > 0:
                W("vector", ac, exp_done[qc - 1])
            R("vector", lambda e, psv=psv, ks=ks: e.tensor_scalar(
                ssb[:, ks * 512:(ks + 1) * 512], psv, 1.0 / 32.0, None, ALU.mult), dv)
            sc_ps_free[ping] = dv.n

    def p2_softmax(qc):
        R("vector", lambda e: e.tensor_reduce(sl(NEGMAX), ssb, AX.X, ALU.max, negate=True), dv)
        n_nm = dv.n
        W("scalar", dv, n_nm)
        if qc > 0:
            W("scalar", pe, ptpe_done[qc - 1])
        R("scalar", lambda e: e.activation(
            p_sb, ssb, AF.Exp, bias=sl(NEGMAX), accum_out=sl(SUMEXP)), ac)
        exp_done[qc] = ac.n
        W("vector", ac, exp_done[qc])
        R("vector", lambda e: e.reciprocal(sl(RINV), sl(SUMEXP)), dv)
        rinv_done[qc] = dv.n

    def p2_pt_av(qc):
        for g in range(4):
            bnk = 4 + (g % 2)
            W("tensor", ac, exp_done[qc])
            W("tensor", dv, pt2_free[g % 2])
            for j in range(8):
                kc = g * 8 + j
                R("tensor", lambda e, kc=kc, j=j, bnk=bnk: e.transpose(
                    bankb(bnk)[:, j * 128:(j + 1) * 128],
                    p_sb[:, kc * 128:(kc + 1) * 128], ident),
                  pe if j == 7 else None)
            n_t = pe.n
            W("vector", pe, n_t)
            for j in range(8):
                kc = g * 8 + j
                R("vector", lambda e, kc=kc, j=j, bnk=bnk: e.tensor_copy(
                    pt_sb[:, kc * 128:(kc + 1) * 128],
                    bankb(bnk)[:, j * 128:(j + 1) * 128]),
                  dv if j == 7 else None)
            pt2_free[g % 2] = dv.n
        ptpe_done[qc] = pe.n
        ptsb_done[qc] = dv.n
        for es in range(2):
            psv = bank(2 + es)
            W("tensor", dv, ptsb_done[qc])
            W("tensor", ac, av_ps_free[es])
            for kc in range(NKC):
                R("tensor", lambda e, psv=psv, kc=kc, es=es: e.matmul(
                    psv, pt_sb[:, kc * 128:(kc + 1) * 128],
                    v_sb[:, kc * 1024 + es * 512: kc * 1024 + es * 512 + 512],
                    start=(kc == 0), stop=(kc == NKC - 1)),
                  pe if kc == NKC - 1 else None)
            n_av = pe.n
            W("scalar", pe, n_av)
            W("scalar", dv, rinv_done[qc])
            if qc > 0:
                W("scalar", pe, avTpe_done[qc - 1])
            R("scalar", lambda e, psv=psv, es=es: e.activation(
                av_sb[:, es * 512:(es + 1) * 512], psv, AF.Identity, scale=sl(RINV)), ac)
            av_ps_free[es] = ac.n
        avsb_done[qc] = ac.n
        W("tensor", ac, avsb_done[qc])
        W("tensor", dv, pt2_free[0])
        for ec in range(8):
            R("tensor", lambda e, ec=ec: e.transpose(
                bankb(4)[:, ec * 128:(ec + 1) * 128],
                av_sb[:, ec * 128:(ec + 1) * 128], ident),
              pe if ec == 7 else None)
        avTpe_done[qc] = pe.n
        W("vector", pe, avTpe_done[qc])
        for ec in range(8):
            R("vector", lambda e, ec=ec, qc=qc: e.tensor_copy(
                av_fm[:, ec * 1024 + qc * 128: ec * 1024 + qc * 128 + 128],
                bankb(4)[:, ec * 128:(ec + 1) * 128]),
              dv if ec == 7 else None)
        pt2_free[0] = dv.n
        avT_done[qc] = dv.n

    p2_scores_mm(0)
    p2_softmax(0)
    for qc in range(1, 8):
        p2_scores_mm(qc)
        if qc == 7 and phases >= 4:
            W("gpsimd", pe, pe.n)
            for ec in range(8):
                R("gpsimd", lambda e, ec=ec: e.dma_start(
                    out=wo_sb[:, ec * 4096:(ec + 1) * 4096], in_=wo_r[ec],
                    max_dma_last_dim=8192), wld, 16)
        p2_pt_av(qc - 1)
        p2_softmax(qc)
    p2_pt_av(7)
    p2_pe, p2_dv, p2_ac = pe.n, dv.n, ac.n
    if phases < 4:
        return _finish(nc, bld, tap_list, st)

    W("gpsimd", ac, p2_ac)
    W("gpsimd", pe, p2_pe)
    W("gpsimd", dv, p2_dv)
    R("gpsimd", lambda e: e.dma_start(out=bo_sb, in_=bo_r, max_dma_last_dim=8192), wld, 16)
    wo_ld = wld.n
    pre5 = {}
    for pfs in range(2):
        for j in range(8):
            pre5[pfs] = R("gpsimd", lambda e, j=j, fc=pfs * 8 + j, ping=pfs: e.dma_start(
                out=dwn5[ping][:, j * 2048:(j + 1) * 2048],
                in_=dwn_r[fc][:, 0:2048], max_dma_last_dim=8192), wld, 16)
    W("sync", pe, p2_pe)
    W("sync", ac, p2_ac)
    W("sync", dv, p2_dv)
    R("sync", lambda e: e.dma_start(out=ones16, in_=ones16_in), ld, 16)
    ones_ld = ld.n
    wo_ld2 = wo_ld

    x3_free = [0, 0]
    op_free = [0, 0]
    wo_ps_free = [0, 0]
    x3_ld = [0, 0]
    for tc in range(NTC):
        xv = x3[tc % 2]
        W("sync", dv, x3_free[tc % 2])
        x3_ld[tc % 2] = R("sync", lambda e, xv=xv, tc=tc: e.dma_start(
            out=xv, in_=xb[tc * 128:(tc + 1) * 128, :]), ld, 16)
        for half in range(2):
            ping = (tc * 2 + half) % 2
            psv = bank(ping * 4, 4)
            W("tensor", wld, wo_ld2)
            W("tensor", ld, ones_ld)
            W("tensor", dv, wo_ps_free[ping])
            W("tensor", dv, avT_done[7])
            W("tensor", ac, p2_ac)
            for sp in range(4):
                tgt = psv[:, sp * 512:(sp + 1) * 512]
                coff = half * 2048 + sp * 512
                for ec in range(8):
                    R("tensor", lambda e, t=tgt, ec=ec, tc=tc, coff=coff: e.matmul(
                        t, av_fm[:, ec * 1024 + tc * 128: ec * 1024 + tc * 128 + 128],
                        wo_sb[:, ec * 4096 + coff: ec * 4096 + coff + 512],
                        start=(ec == 0), stop=False))
                R("tensor", lambda e, t=tgt, coff=coff: e.matmul(
                    t, ones16, bo_sb[0:1, coff:coff + 512], start=False, stop=True),
                  pe if sp == 3 else None)
            n_mm = pe.n
            opv = opb[ping]
            W("vector", pe, n_mm)
            W("vector", ld, x3_ld[tc % 2])
            W("vector", st, op_free[ping])
            R("vector", lambda e, opv=opv, psv=psv, xv=xv, half=half: e.tensor_tensor(
                opv, psv, xv[:, half * 2048:(half + 1) * 2048], ALU.add), dv)
            wo_ps_free[ping] = dv.n
            n_op = dv.n
            W("sync", dv, n_op)
            R("sync", lambda e, opv=opv, tc=tc, half=half: e.dma_start(
                out=op_dr[tc][:, half * 2048:(half + 1) * 2048], in_=opv), st, 16)
            op_free[ping] = st.n
        x3_free[tc % 2] = dv.n
    p3_pe, p3_dv, p3_ac, p3_st = pe.n, dv.n, ac.n, st.n
    if phases < 5:
        return _finish(nc, bld, tap_list, st)

    W("sync", pe, p3_pe)
    W("sync", ac, p3_ac)
    W("sync", dv, p3_dv)
    W("sync", st, p3_st)
    for ch in range(2):
        R("sync", lambda e, ch=ch: e.dma_start(
            out=ds_halfs[ch], in_=ds_r[0][ch * 2048:(ch + 1) * 2048].partition_broadcast(128)),
          ld, 16)
    ds_ld = ld.n
    W("gpsimd", pe, p3_pe)
    W("gpsimd", dv, p3_dv)
    W("gpsimd", st, p3_st)
    W("tensor", ac, p3_ac)
    W("tensor", dv, p3_dv)

    NSUP = (NFC + 7) // 8
    dw_ps_free = [0, 0]
    h5_free = [0, 0]
    d5_free = [0, 0]
    h5_ld = [0, 0]
    d5_ld = [0, 0]
    opt_used = [0, 0]
    ot_st = [0, 0]

    def emit_final_tc(ch, tc, acc_done):
        if True:
            ping = tc % 2
            opv = opt5[ping]
            W("sync", dv, opt_used[ping])
            W("sync", st, ot_st[ping])
            n_o = R("sync", lambda e, opv=opv, tc=tc, ch=ch: e.dma_start(
                out=opv, in_=op_dr[tc][:, ch * 2048:(ch + 1) * 2048]), ld, 16)
            W("vector", dv, acc_done)
            W("vector", ld, max(n_o, ds_ld))
            W("vector", st, ot_st[ping])
            R("vector", lambda e, tc=tc, ch=ch: e.tensor_tensor(
                acc[:, tc * 2048:(tc + 1) * 2048],
                ds_halfs[ch], acc[:, tc * 2048:(tc + 1) * 2048], ALU.mult), None)
            n_f = R("vector", lambda e, opv=opv, tc=tc: e.tensor_tensor(
                opv, acc[:, tc * 2048:(tc + 1) * 2048], opv, ALU.add), dv)
            opt_used[ping] = n_f
            W("sync", dv, n_f)
            R("sync", lambda e, opv=opv, tc=tc, ch=ch: e.dma_start(
                out=out_dr[tc * 128:(tc + 1) * 128, ch * 2048:(ch + 1) * 2048], in_=opv), st, 16)
            ot_st[ping] = st.n

    acc_done_prev = 0
    for ch in range(2):
        for fs in range(NSUP):
            nf = min(8, NFC - fs * 8)
            ping = fs % 2
            W("sync", pe, h5_free[ping])
            W("sync", st, p4_st)
            for j in range(nf):
                h5_ld[ping] = R("sync", lambda e, j=j, fc=fs * 8 + j, ping=ping: e.dma_start(
                    out=h5[ping][:, j * 1024:(j + 1) * 1024], in_=h_dr[fc]), ld, 16)
            if ch == 0 and fs < 2:
                d5_ld[ping] = pre5[fs]
            else:
                W("gpsimd", pe, d5_free[ping])
                for j in range(nf):
                    d5_ld[ping] = R("gpsimd", lambda e, j=j, fc=fs * 8 + j, ping=ping, ch=ch:
                                    e.dma_start(
                        out=dwn5[ping][:, j * 2048:(j + 1) * 2048],
                        in_=dwn_r[fc][:, ch * 2048:(ch + 1) * 2048],
                        max_dma_last_dim=8192), wld, 16)
            grp_first = True
            grp_last = True
            for tc in range(NTC):
                if ch == 1 and fs == 0:
                    emit_final_tc(0, tc, acc_done_prev)
                pping = tc % 2
                psv = bank(pping * 4, 4)
                W("tensor", ld, h5_ld[ping])
                W("tensor", wld, d5_ld[ping])
                if grp_first:
                    W("tensor", dv, dw_ps_free[pping])
                for j in range(nf):
                    for sp in range(4):
                        R("tensor", lambda e, psv=psv, j=j, sp=sp, tc=tc, ping=ping,
                          s0=(grp_first and j == 0), s1=(grp_last and j == nf - 1): e.matmul(
                            psv[:, sp * 512:(sp + 1) * 512],
                            h5[ping][:, j * 1024 + tc * 128: j * 1024 + tc * 128 + 128],
                            dwn5[ping][:, j * 2048 + sp * 512: j * 2048 + sp * 512 + 512],
                            start=s0, stop=s1),
                          pe if (j == nf - 1 and sp == 3) else None)
                n_mm = pe.n
                if grp_last:
                    W("vector", pe, n_mm)
                    if fs == 0:
                        W("vector", st, p4_st)
                        R("vector", lambda e, psv=psv, tc=tc: e.tensor_copy(
                            acc[:, tc * 2048:(tc + 1) * 2048], psv), dv)
                    else:
                        R("vector", lambda e, psv=psv, tc=tc: e.tensor_tensor(
                            acc[:, tc * 2048:(tc + 1) * 2048], psv,
                            acc[:, tc * 2048:(tc + 1) * 2048], ALU.add), dv)
                    dw_ps_free[pping] = dv.n
            h5_free[ping] = pe.n
            d5_free[ping] = pe.n
        acc_done_prev = dv.n
        if ch == 0 and acc_tap is not None:
            W("sync", dv, acc_done_prev)
            R("sync", lambda e: e.dma_start(out=acc_tap, in_=acc), st, 16)
            R("sync", lambda e: e.dma_start(out=ds_tap, in_=ds_halfs[0]), st, 16)
            W("sync", st, st.n)
    for tc in range(NTC):
        emit_final_tc(1, tc, acc_done_prev)
    return _finish(nc, bld, tap_list, st)


def _finish(nc, bld, tap_list, st):
    for t, src in tap_list:
        bld.wait("sync", st, st.n)
        bld.run("sync", lambda e, t=t, src=src: e.dma_start(out=t, in_=src), st, 16)
    bld.wait("sync", st, st.n)

    from contextlib import ExitStack
    with ExitStack() as es:
        for s in bld.sems.values():
            s.h = es.enter_context(nc.semaphore(s.name))
        block = es.enter_context(nc.Block())
        for name in ENGINES:
            def body(e, name=name):
                for f in bld.prog[name]:
                    f(e)
            getattr(block, name)(body)
    return nc


def _relayout(Wqkv, bqkv, Wo, bo, gate_w, gate_s, up_w, up_s, down_w, down_s):
    d = {}
    d["wqkv_r"] = np.ascontiguousarray(Wqkv.reshape(NKC, 128, 24, 128).transpose(2, 1, 0, 3))
    d["bqkv_r"] = np.ascontiguousarray(bqkv.reshape(24, 128).T)
    d["wo_r"] = np.ascontiguousarray(Wo.reshape(8, 128, D))
    d["bo_r"] = np.ascontiguousarray(bo.reshape(1, D))
    d["gate_r"] = np.ascontiguousarray(gate_w.reshape(NFC, 128, NKC, 128).transpose(0, 3, 2, 1))
    d["up_r"] = np.ascontiguousarray(up_w.reshape(NFC, 128, NKC, 128).transpose(0, 3, 2, 1))
    d["gs_r"] = np.ascontiguousarray(gate_s.reshape(NFC, 128).T)
    d["us_r"] = np.ascontiguousarray(up_s.reshape(NFC, 128).T)
    d["dwn_r"] = np.ascontiguousarray(down_w.T.reshape(NFC, 128, D))
    d["ds_r"] = np.ascontiguousarray(down_s.reshape(1, D))
    d["ident_in"] = np.eye(128, dtype=ml_dtypes.bfloat16)
    d["eps_in"] = np.full((128, 1), 1e-5, np.float32)
    d["ones16_in"] = np.ones((1, 128), ml_dtypes.bfloat16)
    return d


def kernel(L0_attn_x, L0_attn_Wqkv, L0_attn_bqkv, L0_attn_Wo, L0_attn_bo,
           L0_ffn_gate_w, L0_ffn_gate_s, L0_ffn_up_w, L0_ffn_up_s,
           L0_ffn_down_w, L0_ffn_down_s):
    x = np.asarray(L0_attn_x, dtype=np.float32)
    shared = _relayout(np.asarray(L0_attn_Wqkv), np.asarray(L0_attn_bqkv),
                       np.asarray(L0_attn_Wo), np.asarray(L0_attn_bo),
                       np.asarray(L0_ffn_gate_w), np.asarray(L0_ffn_gate_s),
                       np.asarray(L0_ffn_up_w), np.asarray(L0_ffn_up_s),
                       np.asarray(L0_ffn_down_w), np.asarray(L0_ffn_down_s))
    nc = build_nc()
    in_maps = []
    for c in range(8):
        b, rot = c // 4, c % 4
        xr = np.concatenate([x[b, rot * SC:], x[b, :rot * SC]], axis=0)
        im = dict(shared)
        im["xb"] = np.ascontiguousarray(xr)
        in_maps.append(im)
    res = run_bass_kernel_spmd(nc, in_maps, core_ids=list(range(8)))
    out = np.empty((2, NSL * SC, D), np.float32)
    for c in range(8):
        b, rot = c // 4, c % 4
        out[b, rot * SC:(rot + 1) * SC] = np.asarray(res.results[c]["out"])
    return out
